# version 01 (final)
# baseline (speedup 1.0000x reference)
import numpy as np
import ml_dtypes

import concourse.bass as bass
import concourse.mybir as mybir
from concourse.bass_utils import run_bass_kernel_spmd

F32 = mybir.dt.float32
BF16 = mybir.dt.bfloat16
AF = mybir.ActivationFunctionType
ALU = mybir.AluOpType
AXX = mybir.AxisListType.X
BF = ml_dtypes.bfloat16

DM = 4096
NFC = DM // 128
T = 1024
HALO = 128
NT = T + HALO
NH, HD, SS, NG, L = 128, 64, 128, 8, 128
INTER = NH * HD
CONVD = INTER + 2 * NG * SS
PROJ = INTER + CONVD + NH
NCH = T // L
OC_X = INTER // 128
OC_GATE = INTER // 128
NCV = OC_X + 2 * NG
DT_MAX = 100.0
DT_MIN = 1e-3
EPS = 1e-5

DEBUG_DUMPS = []
STOP_AFTER = None
P3N = NCH


class Sem:
    def __init__(self, h):
        self.h = h
        self.n = 0

    def inc(self, instr, k=1):
        instr.then_inc(self.h, k)
        self.n += k
        return instr


class Alloc:
    def __init__(self, nc):
        self.nc = nc
        self.live = {}
        self.order = {"left": [], "right": []}

    def __call__(self, name, shape, dt=F32, side="left"):
        cm = self.nc.sbuf_tensor(name, shape, dt, side=side)
        t = cm.__enter__()
        self.live[name] = (cm, side)
        self.order[side].append(name)
        return t

    def free(self, *names):
        nm = set(names)
        for side in ("left", "right"):
            while self.order[side] and self.order[side][-1] in nm:
                n = self.order[side].pop()
                nm.discard(n)
                self.live.pop(n)[0].__exit__(None, None, None)
        assert not nm, f"non-LIFO free: {nm}"

    def close(self):
        for side in ("left", "right"):
            while self.order[side]:
                n = self.order[side].pop()
                self.live.pop(n)[0].__exit__(None, None, None)


def build(mode):
    pre = mode == "pre"
    nc = bass.Bass("TRN2", target_bir_lowering=False,
                   detect_race_conditions=False)

    def di(name, shape, dt=F32):
        return nc.dram_tensor(name, shape, dt, kind="ExternalInput").ap()

    xT = di("xT", [DM, NT])
    w_in = di("w_in", [PROJ // 128, 128, NFC, 128])
    w_ldn = di("w_ldn", [4, 128, NFC, 128])
    w_wq = di("w_wq", [1, 128, NFC, 128])
    convw = di("convw", [128, NCV * 4])
    convb = di("convb", [128, NCV])
    dt_bias = di("dt_bias", [128, 1])
    A_col = di("A_col", [128, 1])
    identb = di("identb", [128, 128], BF16)
    identf = di("identf", [128, 128])
    maskT1 = di("maskT1", [128, 128], BF16)
    maskT2 = di("maskT2", [128, 128], BF16)
    onesf = di("onesf", [1, 128])
    onescol = di("onescol", [128, 1])
    fcs = di("fcs", [128, 8])
    onesm = di("onesm", [128, 8])
    if not pre:
        w_org = di("w_org", [NFC, 128, NFC, 128])
        w_out = di("w_out", [NFC, 128, INTER // 128, 128])
        w_lup = di("w_lup", [NFC, 128, 4, 128])
        keysT = di("keysT", [128, 8])
        D_feat = di("D_feat", [128, OC_X])
        normw = di("normw", [128, OC_X])
        contribs = di("contribs", [4, 128, INTER], BF16)
        glogs = di("glogs", [4, 128])
        qparts = di("qparts", [4, 128])
        kselb = di("kselb", [128, 12])
        mselb = di("mselb", [128, 3])

    if pre:
        contrib_o = nc.dram_tensor("contrib_o", [128, INTER], BF16,
                                   kind="ExternalOutput").ap()
        logG_o = nc.dram_tensor("logG_o", [128, 1], F32,
                                kind="ExternalOutput").ap()
        qpart_o = nc.dram_tensor("qpart_o", [128, 1], F32,
                                 kind="ExternalOutput").ap()
    else:
        outT = nc.dram_tensor("outT", [DM, T], F32, kind="ExternalOutput").ap()

    def dram(name, shape, dt=F32):
        return nc.dram_tensor(name, shape, dt, kind="Internal").ap()

    x_bnc = dram("x_bnc", [INTER, T], BF16)
    B_bnc = dram("B_bnc", [NG * SS, T], BF16)
    C_bnc = dram("C_bnc", [NG * SS, T], BF16)
    hc_bnc = dram("hc_bnc", [128, 4 * T], BF16)
    cs_stage = dram("cs_stage", [NCH, 128, 128], F32)
    cd_stage = dram("cd_stage", [NCH, 128], F32)
    if not pre:
        gate_bnc = dram("gate_bnc", [OC_GATE, 128, T], BF16)
        y_bnc = dram("y_bnc", [T, INTER], BF16)
        outpart = dram("outpart", [NFC, 128, T], F32)
        w_stage = dram("w_stage", [128, 3], F32)
        cf_stage = dram("cf_stage", [8, 1], F32)

    al = Alloc(nc)
    psum_cms = [nc.psum_tensor(f"p{i}", [128, 512], F32) for i in "ABCDEFGH"]
    pA, pB, pC, pD, pE, pF, pG, pH = (cm.__enter__() for cm in psum_cms)
    sem_cms = []

    def mksem(name):
        cm = nc.semaphore(name)
        sem_cms.append(cm)
        return Sem(cm.__enter__())

    s_mm = mksem("s_mm")
    s_ev = mksem("s_ev")
    s_ac = mksem("s_ac")
    s_gp = mksem("s_gp")
    s_c = mksem("s_c")
    s_cw = mksem("s_cw")
    s_xv = [mksem("s_xa"), mksem("s_xb")]
    s_wv = [mksem("s_wa"), mksem("s_wb")]
    s_trv = [mksem("s_tra"), mksem("s_trb")]
    s_csv = [mksem("s_csa"), mksem("s_csb")]
    s_cdr = mksem("s_cdr")
    s_p2 = mksem("s_p2")
    s_kT = mksem("s_kT")
    s_gl = mksem("s_gl")
    s_cb = mksem("s_cb")
    s_wr = mksem("s_wr")
    s_ce = mksem("s_ce")
    s_bcf = mksem("s_bcf")
    s_try = mksem("s_try")
    s_gch = mksem("s_gch")
    s_xfc = mksem("s_xfc")
    s_xre = mksem("s_xre")
    s_opv = [mksem("s_opa"), mksem("s_opb")]
    s_hcs = mksem("s_hcs")
    s_bwv = [mksem("s_bwa"), mksem("s_bwb")]
    s_stg = mksem("s_stg")
    s_sty = mksem("s_sty")
    s_stv = [mksem("s_so_a"), mksem("s_so_b")]
    s_ouv = [mksem("s_ou_a"), mksem("s_ou_b")]
    s_out = mksem("s_out")
    s_dmp = mksem("s_dmp")
    ALL_STORES = [s_bwv[0], s_bwv[1], s_stg, s_sty, s_stv[0], s_stv[1],
                  s_ouv[0], s_ouv[1], s_out, s_dmp]

    blk_cm = nc.Block()
    blk = blk_cm.__enter__()

    onesf_s = al("onesf_s", [1, 128])
    onescol_s = al("onescol_s", [128, 1])
    identb_s = al("identb_s", [128, 128], BF16)
    identf_s = al("identf_s", [128, 128])
    maskT1_s = al("maskT1_s", [128, 128], BF16)
    maskT2_s = al("maskT2_s", [128, 128], BF16)
    dtb_s = al("dtb_s", [128, 1])
    A_s = al("A_s", [128, 1])
    rstdrow = al("rstdrow", [1, NT])
    nmr_row = al("nmr_row", [1, NT])
    cdrow = al("cdrow", [1, 128])
    logG = al("logG", [128, 1])
    scolf = al("scolf", [128, NFC])
    scol = al("scol", [128, NFC], BF16)
    qp_s = al("qp_s", [128, 1])
    fcs_s = al("fcs_s", [128, 8])
    onesm_s = al("onesm_s", [128, 8])
    if not pre:
        keysT_s = al("keysT_s", [128, 8], BF16, side="right")
        Df_s = al("Df_s", [128, OC_X], side="right")
        nw_s = al("nw_s", [128, OC_X], side="right")

    C_EPS, C_NHALF, C_NEG1, C_INVDM = 0, 1, 2, 3
    C_DTMAX, C_DTMIN, C_QS, C_INV1024 = 4, 5, 6, 7

    def fc(i):
        return fcs_s[:, i:i + 1]

    def finish():
        @blk.sync
        def _(sync):
            for s in ALL_STORES:
                if s.n:
                    sync.wait_ge(s.h, s.n)
        al.close()
        blk_cm.__exit__(None, None, None)
        for cm in reversed(psum_cms):
            cm.__exit__(None, None, None)
        for cm in reversed(sem_cms):
            cm.__exit__(None, None, None)
        return nc

    def dump(name, ap_src):
        if name not in DEBUG_DUMPS:
            return
        shp = list(ap_src.shape)
        d = nc.dram_tensor("dbg_" + name, shp, ap_src.dtype,
                           kind="ExternalOutput").ap()

        @blk.sync
        def _(sync):
            sync.wait_ge(s_ev.h, s_ev.n)
            sync.wait_ge(s_ac.h, s_ac.n)
            sync.wait_ge(s_mm.h, s_mm.n)
            sync.wait_ge(s_gp.h, s_gp.n)
            s_dmp.inc(sync.dma_start(
                out=d[tuple(slice(0, s) for s in shp)], in_=ap_src), 16)

    @blk.sync
    def _(sync):
        pairs = [(onesf_s, onesf), (onescol_s, onescol), (identb_s, identb),
                 (identf_s, identf), (maskT1_s, maskT1), (maskT2_s, maskT2),
                 (dtb_s, dt_bias), (A_s, A_col), (fcs_s, fcs),
                 (onesm_s, onesm)]
        if not pre:
            pairs += [(Df_s, D_feat), (nw_s, normw)]
        for dst, src in pairs:
            s_c.inc(sync.dma_start(out=dst[:], in_=src[:, :]), 16)

    @blk.gpsimd
    def _(gp):
        if not pre:
            s_kT.inc(gp.dma_start(out=keysT_s[:], in_=keysT[:, :]), 16)
    consts_c = s_c.n
    kT_c = s_kT.n

    if not pre:
        h_lora = al("h_lora", [128, 4 * T], BF16)
    xnT = al("xnT", [128, NFC * NT], BF16)
    xTc = [al(f"xTc{i}", [128, NT]) for i in range(2)]
    xsq = al("xsq", [128, NT])
    rb_bc = al("rb_bc", [128, NT])
    mb_bc = al("mb_bc", [128, NT])
    murow = al("murow", [1, NT])
    varrow = al("varrow", [1, NT])
    xb_b = al("xb_b", [128, NT], BF16)
    xsq_b = al("xsq_b", [128, NT], BF16)

    P0SL = [(0, 512), (512, 1024), (1024, NT)]
    P0ROW = [0, 32, 64]

    def xwait(eng, base, fcc):
        eng.wait_ge(s_xv[fcc % 2].h, base[fcc % 2] + 16 * (fcc // 2 + 1))

    def emit_ln_stats_and_xnT(first_time):
        mm0, ac0, ev0 = s_mm.n, s_ac.n, s_ev.n

        if first_time:
            xb0 = [s_xv[0].n, s_xv[1].n]

            @blk.sync
            def _(sync):
                for fcc in range(NFC):
                    if fcc >= 2:
                        sync.wait_ge(s_mm.h, mm0 + fcc - 1)
                    s_xv[fcc % 2].inc(sync.dma_start(
                        out=xTc[fcc % 2][:],
                        in_=xT[fcc * 128:(fcc + 1) * 128, :]), 16)

            @blk.scalar
            def _(sc):
                for fcc in range(NFC):
                    xwait(sc, xb0, fcc)
                    if fcc >= 1:
                        sc.wait_ge(s_mm.h, mm0 + fcc)
                    s_ac.inc(sc.activation(xsq_b[:], xTc[fcc % 2][:],
                                           AF.Square))

            @blk.vector
            def _(v):
                for fcc in range(NFC):
                    xwait(v, xb0, fcc)
                    if fcc >= 1:
                        v.wait_ge(s_mm.h, mm0 + fcc)
                    s_ev.inc(v.tensor_copy(xb_b[:], xTc[fcc % 2][:]))

            @blk.tensor
            def _(tn):
                tn.wait_ge(s_c.h, consts_c)
                ob = maskT1_s[:, 127:128]
                for fcc in range(NFC):
                    tn.wait_ge(s_ac.h, ac0 + fcc + 1)
                    tn.wait_ge(s_ev.h, ev0 + fcc + 1)
                    first, last = fcc == 0, fcc == NFC - 1
                    for (a, b), r in zip(P0SL, P0ROW):
                        tn.matmul(pA[r:r + 1, 0:b - a], ob,
                                  xb_b[:, a:b], start=first, stop=last)
                    for i, ((a, b), r) in enumerate(zip(P0SL, P0ROW)):
                        mi = tn.matmul(pB[r:r + 1, 0:b - a], ob,
                                       xsq_b[:, a:b], start=first, stop=last)
                        if i == 2:
                            s_mm.inc(mi)
            p0_mm = s_mm.n

            @blk.vector
            def _(v):
                v.wait_ge(s_mm.h, p0_mm)
                for (a, b), r in zip(P0SL, P0ROW):
                    v.tensor_scalar(murow[0:1, a:b], pA[r:r + 1, 0:b - a],
                                    fc(C_INVDM)[0:1, :], None, ALU.mult)
                    v.tensor_scalar(varrow[0:1, a:b], pB[r:r + 1, 0:b - a],
                                    fc(C_INVDM)[0:1, :], None, ALU.mult)
                v.tensor_tensor(rstdrow[0:1, :], murow[0:1, :],
                                murow[0:1, :], ALU.mult)
                s_ev.inc(v.tensor_tensor(varrow[0:1, :], varrow[0:1, :],
                                         rstdrow[0:1, :], ALU.subtract))
            stats_ev = s_ev.n

            @blk.scalar
            def _(sc):
                sc.wait_ge(s_ev.h, stats_ev)
                sc.activation(rstdrow[0:1, :], varrow[0:1, :], AF.Ln,
                              bias=fc(C_EPS)[0:1, :])
                s_ac.inc(sc.activation(rstdrow[0:1, :], rstdrow[0:1, :],
                                       AF.Exp, scale=fc(C_NHALF)[0:1, :]))
            rstd_ac = s_ac.n

            @blk.vector
            def _(v):
                v.wait_ge(s_ac.h, rstd_ac)
                s_ev.inc(v.scalar_tensor_tensor(
                    nmr_row[0:1, :], murow[0:1, :], fc(C_NEG1)[0:1, :],
                    rstdrow[0:1, :], ALU.mult, ALU.mult))

        nmr_ev = s_ev.n
        bc_mm0 = s_mm.n

        @blk.tensor
        def _(tn):
            tn.wait_ge(s_ev.h, nmr_ev)
            if not first_time:
                tn.wait_ge(s_c.h, consts_c)
            for i, (a, b) in enumerate(P0SL):
                if i >= 1:
                    tn.wait_ge(s_ev.h, nmr_ev + i)
                tn.matmul(pC[:, 0:b - a], onesf_s[:], rstdrow[0:1, a:b],
                          start=True, stop=True)
                s_mm.inc(tn.matmul(pD[:, 0:b - a], onesf_s[:],
                                   nmr_row[0:1, a:b], start=True, stop=True))

        @blk.vector
        def _(v):
            for i, (a, b) in enumerate(P0SL):
                v.wait_ge(s_mm.h, bc_mm0 + i + 1)
                v.tensor_copy(rb_bc[:, a:b], pC[:, 0:b - a])
                s_ev.inc(v.tensor_copy(mb_bc[:, a:b], pD[:, 0:b - a]))
        bc_ev = s_ev.n

        xb1 = [s_xv[0].n, s_xv[1].n]

        @blk.sync
        def _(sync):
            sync.wait_ge(s_ev.h, bc_ev)
            for fcc in range(NFC):
                if fcc >= 2:
                    sync.wait_ge(s_ev.h, bc_ev + fcc - 1)
                s_xv[fcc % 2].inc(sync.dma_start(
                    out=xTc[fcc % 2][:],
                    in_=xT[fcc * 128:(fcc + 1) * 128, :]), 16)

        @blk.vector
        def _(v):
            for fcc in range(NFC):
                xwait(v, xb1, fcc)
                v.tensor_tensor(xsq[:], xTc[fcc % 2][:], rb_bc[:], ALU.mult)
                s_ev.inc(v.tensor_tensor(xnT[:, fcc * NT:(fcc + 1) * NT],
                                         xsq[:], mb_bc[:], ALU.add))
        return s_ev.n

    xnT_ev = emit_ln_stats_and_xnT(True)
    dump("xnT", xnT[:, 0:2 * NT])

    if STOP_AFTER == "p0" and not pre:
        return finish()

    al.free("xsq_b", "xb_b", "varrow", "murow", "mb_bc", "rb_bc", "xsq",
            "xTc1", "xTc0")

    convw_s = al("convw_s", [128, NCV * 4])
    convb_s = al("convb_s", [128, NCV])
    wbuf = [al(f"wbuf{i}", [128, 4096], BF16) for i in range(2)]
    xbcbuf = [al(f"xbcbuf{i}", [128, T + 3], BF16) for i in range(2)]
    convbuf = [al(f"convbuf{i}", [128, T], BF16) for i in range(2)]
    dtraw = al("dtraw", [128, T])
    dtc_s = al("dtc_s", [128, T])
    v_s = al("v_s", [128, T])
    cs_s = al("cs_s", [128, T])
    ecs_s = al("ecs_s", [128, T])
    ecsT_a = al("ecsT_a", [128, T], side="right")
    vT_a = al("vT_a", [128, T], side="right")
    g2T_a = al("g2T_a", [128, T], BF16, side="right")

    n_xbc = NCV if not pre else OC_X + NG
    OCS = [("xbc", j, w_in, 64 + j) for j in range(n_xbc)]
    OCS += [("dt", 0, w_in, 144)]
    if not pre:
        OCS += [("gate", j, w_in, j) for j in range(OC_GATE)]
        OCS += [("h", j, w_ldn, j) for j in range(4)]
    else:
        OCS += [("q", 0, w_wq, 0)]
    TL_XBC = [(125, 637), (637, 1149), (1149, 1152)]
    TL_STD = [(128, 640), (640, 1152)]

    def tiles_of(kind):
        return (TL_XBC if kind == "xbc" else
                [(0, 1)] if kind == "q" else TL_STD)

    mm1, ev1, ac1 = s_mm.n, s_ev.n, s_ac.n
    wb0 = [s_wv[0].n, s_wv[1].n]
    bw0 = [s_bwv[0].n, s_bwv[1].n]

    def p1_mm_after(i):
        return mm1 + sum(len(tiles_of(k[0])) for k in OCS[:i + 1])

    def p1_mm_tile(i, ti):
        return mm1 + sum(len(tiles_of(k[0])) for k in OCS[:i]) + ti + 1

    def p1_ev_tile(i, ti):
        n = ev1 + (1 if pre else 0)
        for k in range(i):
            n += len(tiles_of(OCS[k][0]))
            if OCS[k][0] == "xbc":
                n += 1
        return n + ti + 1

    def p1_ev_after(i):
        n = ev1 + (1 if pre else 0)
        for k in range(i + 1):
            n += len(tiles_of(OCS[k][0]))
            if OCS[k][0] == "xbc":
                n += 1
        return n

    def p1_ac_after(i):
        n = ac1
        for k in range(i + 1):
            if OCS[k][0] == "xbc":
                n += 1
        return n

    def p1_bw_count(i, par):
        n = 0
        for k in range(i + 1):
            kind = OCS[k][0]
            if (kind == "xbc" or (kind == "gate" and not pre)) \
                    and k % 2 == par:
                n += 1
        return n

    _tile_ord = []
    for _i, _k in enumerate(OCS):
        for _ti in range(len(tiles_of(_k[0]))):
            _tile_ord.append((_i, _ti))

    def p1_tile_ev_by_ord(o):
        i, ti = _tile_ord[o]
        return p1_ev_tile(i, ti)

    @blk.sync
    def _(sync):
        sync.wait_ge(s_ev.h, xnT_ev)
        for dst, srcc in [(convw_s, convw), (convb_s, convb)]:
            s_cw.inc(sync.dma_start(out=dst[:], in_=srcc[:, :]), 16)
    cw_c = s_cw.n

    @blk.gpsimd
    def _(gp):
        for i, (kind, j, wt, wb) in enumerate(OCS):
            if i == 0:
                gp.wait_ge(s_ev.h, xnT_ev)
            if i >= 2:
                gp.wait_ge(s_mm.h, p1_mm_after(i - 2))
            s_wv[i % 2].inc(gp.dma_start(
                out=wbuf[i % 2][:, 0:4096],
                in_=wt[wb, :, :, :].rearrange("p a b -> p (a b)")), 16)

    if pre:
        @blk.vector
        def _(v):
            v.wait_ge(s_ev.h, xnT_ev)
            v.memset(scolf[:], 0.0)
            for fc2 in range(NFC):
                v.tensor_reduce(scolf[:, fc2:fc2 + 1],
                                xnT[:, fc2 * NT + HALO:(fc2 + 1) * NT],
                                AXX, ALU.add)
            v.tensor_scalar(scol[:], scolf[:], 1.0, None, ALU.mult)
            s_ev.inc(v.tensor_copy(scol[:, 24:32], scolf[:, 24:32]))
    scol_ev = s_ev.n

    @blk.tensor
    def _(tn):
        tn.wait_ge(s_ev.h, xnT_ev)
        tn.wait_ge(s_cw.h, cw_c)
        for i, (kind, j, wt, wb) in enumerate(OCS):
            tn.wait_ge(s_wv[i % 2].h, wb0[i % 2] + 16 * (i // 2 + 1))
            if kind == "q":
                tn.wait_ge(s_ev.h, scol_ev)
            for ti, (a, b) in enumerate(tiles_of(kind)):
                ordn = p1_mm_tile(i, ti) - mm1 - 1
                p = (pA, pB)[ordn % 2]
                if ordn >= 2:
                    tn.wait_ge(s_ev.h, p1_tile_ev_by_ord(ordn - 2))
                for fc2 in range(NFC):
                    rhs = (scol[:, fc2:fc2 + 1].broadcast_to([128, 8])
                           if kind == "q"
                           else xnT[:, fc2 * NT + a:fc2 * NT + b])
                    mi = tn.matmul(p[:, 0:(8 if kind == "q" else b - a)],
                                   wbuf[i % 2][:, fc2 * 128:(fc2 + 1) * 128],
                                   rhs, start=(fc2 == 0),
                                   stop=(fc2 == NFC - 1))
                s_mm.inc(mi)

    cw_col = lambda oc, k: convw_s[:, oc * 4 + k:oc * 4 + k + 1]

    @blk.vector
    def _(v):
        for i, (kind, j, wt, wb) in enumerate(OCS):
            for ti, (a, b) in enumerate(tiles_of(kind)):
                ordn = p1_mm_tile(i, ti) - mm1 - 1
                p = (pA, pB)[ordn % 2]
                v.wait_ge(s_mm.h, p1_mm_tile(i, ti))
                if kind == "xbc":
                    if ti == 0 and i >= 2:
                        v.wait_ge(s_bwv[i % 2].h,
                                  bw0[i % 2] + 16 * p1_bw_count(i - 2, i % 2))
                    vi = v.tensor_copy(xbcbuf[i % 2][:, a - 125:b - 125],
                                       p[:, 0:b - a])
                elif kind == "dt":
                    vi = v.tensor_copy(dtraw[:, a - 128:b - 128],
                                       p[:, 0:b - a])
                elif kind == "gate":
                    if ti == 0 and i >= 2:
                        v.wait_ge(s_bwv[i % 2].h,
                                  bw0[i % 2] + 16 * p1_bw_count(i - 2, i % 2))
                    vi = v.tensor_copy(convbuf[i % 2][:, a - 128:b - 128],
                                       p[:, 0:b - a])
                elif kind == "h":
                    vi = v.tensor_copy(
                        h_lora[:, j * T + a - 128:j * T + b - 128],
                        p[:, 0:b - a])
                else:
                    vi = v.tensor_copy(qp_s[:], p[:, 0:1])
                s_ev.inc(vi)
            if kind == "xbc":
                oc = j
                v.tensor_scalar(convbuf[i % 2][:], xbcbuf[i % 2][:, 0:T],
                                cw_col(oc, 0), None, ALU.mult)
                for k in range(1, 4):
                    vi = v.scalar_tensor_tensor(
                        convbuf[i % 2][:], xbcbuf[i % 2][:, k:k + T],
                        cw_col(oc, k), convbuf[i % 2][:], ALU.mult, ALU.add)
                s_ev.inc(vi)

    @blk.scalar
    def _(sc):
        for i, (kind, j, wt, wb) in enumerate(OCS):
            if kind != "xbc":
                continue
            sc.wait_ge(s_ev.h, p1_ev_after(i))
            s_ac.inc(sc.activation(convbuf[i % 2][:], convbuf[i % 2][:],
                                   AF.Silu, bias=convb_s[:, j:j + 1]))

    @blk.sync
    def _(sync):
        for i, (kind, j, wt, wb) in enumerate(OCS):
            if kind == "xbc":
                oc = j
                sync.wait_ge(s_ac.h, p1_ac_after(i))
                if oc < OC_X:
                    dst = x_bnc[oc * 128:(oc + 1) * 128, :]
                elif oc < OC_X + NG:
                    g = oc - OC_X
                    dst = B_bnc[g * 128:(g + 1) * 128, :]
                else:
                    g = oc - OC_X - NG
                    dst = C_bnc[g * 128:(g + 1) * 128, :]
                s_bwv[i % 2].inc(sync.dma_start(out=dst,
                                                in_=convbuf[i % 2][:]), 16)
            elif kind == "gate":
                sync.wait_ge(s_ev.h, p1_ev_after(i))
                s_bwv[i % 2].inc(sync.dma_start(
                    out=gate_bnc[j, :, :], in_=convbuf[i % 2][:]), 16)

    dump("qp", qp_s[:])

    dt_i = next(i for i, k in enumerate(OCS) if k[0] == "dt")
    dt_ev = p1_ev_after(dt_i)

    @blk.scalar
    def _(sc):
        sc.wait_ge(s_ev.h, dt_ev)
        sc.activation(dtc_s[:], dtraw[:], AF.Exp, bias=dtb_s[:])
        s_ac.inc(sc.activation(dtc_s[:], dtc_s[:], AF.Ln, bias=1.0))
    a_sp = s_ac.n

    @blk.vector
    def _(v):
        v.wait_ge(s_ac.h, a_sp)
        v.tensor_scalar_min(dtc_s[:], dtc_s[:], fc(C_DTMAX))
        s_ev.inc(v.tensor_scalar_max(dtc_s[:], dtc_s[:], fc(C_DTMIN)))
    e_dtc = s_ev.n

    @blk.scalar
    def _(sc):
        sc.wait_ge(s_ev.h, e_dtc)
        s_ac.inc(sc.activation(v_s[:], dtc_s[:], AF.Ln))
    a_ln = s_ac.n

    @blk.vector
    def _(v):
        v.wait_ge(s_ac.h, a_ln)
        v.tensor_scalar(dtraw[:], dtc_s[:], A_s[:], None, ALU.mult)
        for c in range(NCH):
            sl = slice(c * L, (c + 1) * L)
            v.tensor_tensor_scan(cs_s[:, sl],
                                 onescol_s[:].broadcast_to([128, L]),
                                 dtraw[:, sl], 0.0, ALU.mult, ALU.add)
        v.memset(logG[:], 0.0)
        s_ev.inc(v.tensor_reduce(logG[:], cs_s[:, L - 1:T:L], AXX, ALU.add))
    e_cs = s_ev.n

    @blk.scalar
    def _(sc):
        sc.wait_ge(s_ev.h, e_cs)
        s_ac.inc(sc.activation(ecs_s[:], cs_s[:], AF.Exp))
    a_ecs = s_ac.n

    @blk.vector
    def _(v):
        v.wait_ge(s_ac.h, a_ecs)
        v.tensor_tensor(v_s[:], v_s[:], cs_s[:], ALU.subtract)
        for c in range(NCH):
            sl = slice(c * L, (c + 1) * L)
            s_ev.inc(v.tensor_scalar(dtraw[:, sl], cs_s[:, sl],
                                     cs_s[:, c * L + L - 1:c * L + L],
                                     None, ALU.subtract))
    e_dec = s_ev.n

    @blk.scalar
    def _(sc):
        for c in range(NCH):
            sl = slice(c * L, (c + 1) * L)
            sc.wait_ge(s_ev.h, e_dec - NCH + c + 1)
            s_ac.inc(sc.activation(dtraw[:, sl], dtraw[:, sl], AF.Exp,
                                   scale=fc(C_NEG1)))
    a_dec = s_ac.n

    @blk.vector
    def _(v):
        for c in range(NCH):
            sl = slice(c * L, (c + 1) * L)
            v.wait_ge(s_ac.h, a_dec - NCH + c + 1)
            s_ev.inc(v.tensor_tensor(dtraw[:, sl], dtraw[:, sl],
                                     dtc_s[:, sl], ALU.mult))
    e_g2 = s_ev.n

    tr0 = s_mm.n

    @blk.tensor
    def _(tn):
        tn.wait_ge(s_ev.h, e_dec)
        tn.wait_ge(s_ac.h, a_ecs)
        for c in range(NCH):
            sl = slice(c * L, (c + 1) * L)
            p = (pC, pD)[c % 2]
            if c >= 2:
                tn.wait_ge(s_ev.h, e_g2 + 3 * (c - 1))
            tn.transpose(p[:, 0:128], v_s[:, sl], identf_s[:])
            tn.transpose(p[:, 128:256], ecs_s[:, sl], identf_s[:])
            tn.wait_ge(s_ev.h, e_g2 - NCH + c + 1)
            s_mm.inc(tn.transpose(p[:, 256:384], dtraw[:, sl], identf_s[:]))

    @blk.vector
    def _(v):
        for c in range(NCH):
            sl = slice(c * L, (c + 1) * L)
            p = (pC, pD)[c % 2]
            v.wait_ge(s_mm.h, tr0 + c + 1)
            v.tensor_copy(vT_a[:, sl], p[:, 0:128])
            v.tensor_copy(ecsT_a[:, sl], p[:, 128:256])
            s_ev.inc(v.tensor_copy(g2T_a[:, sl], p[:, 256:384]), 3)
    e_trans = s_ev.n

    @blk.sync
    def _(sync):
        sync.wait_ge(s_ev.h, e_cs)
        for c in range(NCH):
            s_stg.inc(sync.dma_start(out=cs_stage[c, :, :],
                                     in_=cs_s[:, c * L:(c + 1) * L]), 16)
        sync.wait_ge(s_ac.h, a_ecs)
        for c in range(NCH):
            s_stg.inc(sync.dma_start(
                out=cd_stage[c, :].unsqueeze(1),
                in_=ecs_s[:, c * L + L - 1:c * L + L]), 16)
    st_stage = s_stg.n

    dump("cs", cs_s[:])
    dump("scolf", scolf[:])
    dump("scolb", scol[:])

    if STOP_AFTER == "p1" and not pre:
        return finish()

    al.free("ecs_s", "cs_s", "v_s", "dtc_s", "dtraw", "convbuf1", "convbuf0",
            "xbcbuf1", "xbcbuf0", "wbuf1", "wbuf0", "convb_s", "convw_s",
            "xnT")

    carry = al("carry", [128, INTER], BF16)

    if pre:
        @blk.vector
        def _(v):
            s_ev.inc(v.memset(carry[:], 0.0))
        e_carry0 = s_ev.n
    else:
        qall = al("qall", [4, 128])
        qcol = al("qcol", [128, 1], BF16)
        softb = al("softb", [8, 8])
        coeffs = al("coeffs", [8, 8])
        ce_vec = al("ce_vec", [128, 4])
        kselb_s = al("kselb_s", [128, 12])
        mselb_s = al("mselb_s", [128, 3])
        glog = al("glog", [128, 4])
        lw = al("lw", [128, 3])
        wrow = al("wrow", [1, 3 * 128])
        cbase = al("cbase", [128, INTER], BF16)
        ctmp = al("ctmp", [128, 512])

        @blk.sync
        def _(sync):
            sync.wait_ge(s_mm.h, s_mm.n)
            for dst, srcc in [(qall, qparts), (kselb_s, kselb),
                              (mselb_s, mselb)]:
                s_p2.inc(sync.dma_start(out=dst[:], in_=srcc[:, :]), 16)
        x_p2 = s_p2.n

        @blk.tensor
        def _(tn):
            tn.wait_ge(s_p2.h, x_p2)
            s_mm.inc(tn.matmul(pG[:, 0:1], qall[0:4, :], onescol_s[0:4, :],
                               start=True, stop=True))
        m_q = s_mm.n

        @blk.vector
        def _(v):
            v.wait_ge(s_mm.h, m_q)
            s_ev.inc(v.tensor_scalar(qcol[:], pG[:, 0:1], fc(C_QS), None,
                                     ALU.mult))
        e_q = s_ev.n

        @blk.tensor
        def _(tn):
            tn.wait_ge(s_ev.h, e_q)
            tn.wait_ge(s_kT.h, kT_c)
            s_mm.inc(tn.matmul(pG[0:8, 0:1], keysT_s[:], qcol[:],
                               start=True, stop=True))
        m_sc = s_mm.n

        @blk.vector
        def _(v):
            v.wait_ge(s_mm.h, m_sc)
            s_ev.inc(v.tensor_copy(lw[0:8, 0:1], pG[0:8, 0:1]))
        e_sc8 = s_ev.n
        dump("sc8", lw[0:8, 0:1])

        @blk.scalar
        def _(sc):
            sc.wait_ge(s_ev.h, e_sc8)
            s_ac.inc(sc.activation(softb[0:8, 0:1], lw[0:8, 0:1], AF.Exp))
        a_sm = s_ac.n

        @blk.tensor
        def _(tn):
            tn.wait_ge(s_ac.h, a_sm)
            s_mm.inc(tn.matmul(pG[0:8, 1:2], onesm_s[0:8, 0:8],
                               softb[0:8, 0:1], start=True, stop=True))
        m_sm = s_mm.n

        @blk.vector
        def _(v):
            v.wait_ge(s_mm.h, m_sm)
            s_ev.inc(v.reciprocal(glog[0:8, 0:1], pG[0:8, 1:2]))
        e_rc = s_ev.n

        @blk.scalar
        def _(sc):
            sc.wait_ge(s_ev.h, e_rc)
            s_ac.inc(sc.activation(coeffs[0:8, 0:1], softb[0:8, 0:1],
                                   AF.Copy, scale=glog[0:8, 0:1]))
        a_cf = s_ac.n
        dump("coeffs", coeffs[0:8, 0:1])

        @blk.sync
        def _(sync):
            sync.wait_ge(s_ac.h, a_cf)
            s_stg.inc(sync.dma_start(out=cf_stage[:, :],
                                     in_=coeffs[0:8, 0:1]), 16)
            sync.wait_ge(s_stg.h, s_stg.n)
            for e in range(8):
                half = (e % 2) * 64
                s_ce.inc(sync.dma_start(
                    out=ce_vec[half:half + 64, e // 2:e // 2 + 1],
                    in_=cf_stage[e:e + 1, 0:1].partition_broadcast(64)), 16)
        x_ce = s_ce.n

        @blk.vector
        def _(v):
            v.wait_ge(s_ce.h, x_ce)
            for j in range(4):
                vi = v.tensor_scalar(h_lora[:, j * T:(j + 1) * T],
                                     h_lora[:, j * T:(j + 1) * T],
                                     ce_vec[:, j:j + 1], None, ALU.mult)
            s_ev.inc(vi)
        e_hcs = s_ev.n

        @blk.sync
        def _(sync):
            sync.wait_ge(s_ev.h, e_hcs)
            s_stg.inc(sync.dma_start(out=hc_bnc[:, :], in_=h_lora[:]), 16)

        @blk.sync
        def _(sync):
            for jj in range(4):
                s_gl.inc(sync.dma_start(
                    out=glog[:, jj:jj + 1],
                    in_=glogs[jj, :].unsqueeze(1)), 16)
        x_gl = s_gl.n

        @blk.vector
        def _(v):
            v.wait_ge(s_gl.h, x_gl)
            v.memset(lw[:], 0.0)
            for j in range(3):
                for k in range(4):
                    vi = v.scalar_tensor_tensor(
                        lw[:, j:j + 1], glog[:, k:k + 1],
                        kselb_s[:, 4 * j + k:4 * j + k + 1],
                        lw[:, j:j + 1], ALU.mult, ALU.add)
            s_ev.inc(vi)
        e_lw = s_ev.n

        @blk.scalar
        def _(sc):
            sc.wait_ge(s_ev.h, e_lw)
            s_ac.inc(sc.activation(lw[:], lw[:], AF.Exp))
        a_lw = s_ac.n

        @blk.sync
        def _(sync):
            sync.wait_ge(s_ac.h, a_lw)
            s_stg.inc(sync.dma_start(out=w_stage[:, :], in_=lw[:, 0:3]), 16)
            sync.wait_ge(s_stg.h, s_stg.n)
            s_wr.inc(sync.dma_start(
                out=wrow[0:1, :],
                in_=w_stage.rearrange("a b -> (a b)").unsqueeze(0)), 16)
        x_wr = s_wr.n

        @blk.vector
        def _(v):
            s_ev.inc(v.memset(carry[:], 0.0))
        e_c0 = s_ev.n

        x_cb0 = s_cb.n

        @blk.sync
        def _(sync):
            for j in range(3):
                if j > 0:
                    sync.wait_ge(s_ev.h, e_c0 + 32 * j)
                s_cb.inc(sync.dma_start(out=cbase[:],
                                        in_=contribs[j, :, :]), 16)

        m_cmb0 = s_mm.n

        @blk.tensor
        def _(tn):
            tn.wait_ge(s_wr.h, x_wr)
            tn.wait_ge(s_ev.h, e_c0)
            for j in range(3):
                tn.wait_ge(s_cb.h, x_cb0 + 16 * (j + 1))
                for s16 in range(16):
                    p = (pC, pD)[s16 % 2]
                    q = j * 16 + s16
                    if q >= 2:
                        tn.wait_ge(s_ev.h, e_c0 + 2 * (q - 1))
                    rhs = wrow[0:1, :].rearrange(
                        "o (hh jj) -> o hh jj", jj=3)[
                        0:1, s16 * 8:(s16 + 1) * 8, j:j + 1]
                    rhs = rhs.broadcast_to([1, 8, 64])
                    s_mm.inc(tn.matmul(p[:, 0:512], onesf_s[:], rhs,
                                       start=True, stop=True))

        @blk.vector
        def _(v):
            for j in range(3):
                for s16 in range(16):
                    p = (pC, pD)[s16 % 2]
                    q = j * 16 + s16
                    v.wait_ge(s_mm.h, m_cmb0 + q + 1)
                    cols = slice(s16 * 512, (s16 + 1) * 512)
                    v.tensor_tensor(ctmp[:], cbase[:, cols], p[:, 0:512],
                                    ALU.mult)
                    s_ev.inc(v.nop())
                    s_ev.inc(v.scalar_tensor_tensor(
                        carry[:, cols], ctmp[:], mselb_s[:, j:j + 1],
                        carry[:, cols], ALU.mult, ALU.add))
        e_carry0 = s_ev.n
        dump("carry_in", carry[:, 0:1024])

    if STOP_AFTER == "p2" and not pre:
        return finish()
    if not pre:
        al.free("ctmp", "cbase", "wrow", "lw", "glog", "mselb_s",
                "kselb_s", "ce_vec", "coeffs", "softb", "qcol", "qall")

    xTk = [al(f"xTk{i}", [128, INTER], BF16) for i in range(2)]
    xddT = al("xddT", [128, INTER], BF16)
    BTk = [al(f"BTk{i}", [128, NG * SS], BF16) for i in range(2)]
    if not pre:
        ych = al("ych", [128, INTER], BF16)
        csrow = [al(f"csrow{i}", [1, 1024]) for i in range(2)]
        expD = al("expD", [128, 1024], BF16)
        scob = al("scob", [128, 1024], BF16)
        ytmp = al("ytmp", [128, 512], BF16)
        Bfeat = al("Bfeat", [128, NG * T], BF16, side="right")
        Cfeat = al("Cfeat", [128, NG * T], BF16, side="right")

        @blk.sync
        def _(sync):
            sync.wait_ge(s_bwv[0].h, s_bwv[0].n)
            sync.wait_ge(s_bwv[1].h, s_bwv[1].n)
            for g in range(NG):
                s_bcf.inc(sync.dma_start(
                    out=Bfeat[:, g * T:(g + 1) * T],
                    in_=B_bnc[g * 128:(g + 1) * 128, :]), 16)
                s_bcf.inc(sync.dma_start(
                    out=Cfeat[:, g * T:(g + 1) * T],
                    in_=C_bnc[g * 128:(g + 1) * 128, :]), 16)
        x_bcf = s_bcf.n

    NOCT = 16
    m3, e3, a3 = s_mm.n, s_ev.n, s_ac.n
    tr3 = [s_trv[0].n, s_trv[1].n]
    cs3 = [s_csv[0].n, s_csv[1].n]
    cd3 = s_cdr.n
    sy3 = s_sty.n

    if pre:
        MMC, EVC, ACC = 32, 17, 0

        def mm_oct(c, o, step):
            return m3 + c * MMC + 2 * o + step

        def ev_oct(c, o, step=1):
            return e3 + c * EVC + 1 + o + 1

        def ev_xdd(c):
            return e3 + c * EVC + 1
    else:
        MMC, EVC, ACC = 96, 81, 32

        def mm_quad(c, q):
            o, qi = divmod(q, 2)
            return m3 + c * MMC + 6 * o + qi + 1

        def mm_oct(c, o, step):
            return m3 + c * MMC + 6 * o + step

        def ev_quad(c, q):
            o, qi = divmod(q, 2)
            return e3 + c * EVC + 1 + 5 * o + qi + 1

        def ev_oct(c, o, step):
            return e3 + c * EVC + 1 + 5 * o + step

        def ev_xdd(c):
            return e3 + c * EVC + 1

        def ac_quad(c, q):
            return a3 + c * ACC + q + 1

    @blk.sync
    def _(sync):
        sync.wait_ge(s_stg.h, s_stg.n)
        sync.wait_ge(s_mm.h, m3)
        sync.wait_ge(s_ev.h, e3)
        if pre:
            sync.wait_ge(s_bwv[0].h, s_bwv[0].n)
            sync.wait_ge(s_bwv[1].h, s_bwv[1].n)
        for c in range(P3N):
            if c >= 2:
                sync.wait_ge(s_mm.h, m3 + (c - 1) * MMC)
            s_trv[c % 2].inc(sync.dma_start_transpose(
                xTk[c % 2][:], x_bnc[:, c * L:(c + 1) * L]), 16)
            s_trv[c % 2].inc(sync.dma_start_transpose(
                BTk[c % 2][:], B_bnc[:, c * L:(c + 1) * L]), 16)
            if c >= 1:
                sync.wait_ge(s_mm.h, m3 + c * MMC)
            s_cdr.inc(sync.dma_start(out=cdrow[0:1, :],
                                     in_=cd_stage[c, :].unsqueeze(0)), 16)
            if (not pre) and c >= 1:
                sync.wait_ge(s_ev.h, e3 + c * EVC)
                s_sty.inc(sync.dma_start(
                    out=y_bnc[(c - 1) * L:c * L, :], in_=ych[:]), 16)
            if not pre:
                for rr in range(16):
                    gq = c * 32 + rr * 2
                    if gq >= 4:
                        pc, pq = divmod(gq - 3, 32)
                        sync.wait_ge(s_mm.h, mm_quad(pc, pq))
                    s_csv[rr % 2].inc(sync.dma_start(
                        out=csrow[rr % 2][0:1, :],
                        in_=cs_stage[c].rearrange(
                            "h l -> (h l)").unsqueeze(0)[
                            0:1, rr * 1024:(rr + 1) * 1024]), 16)
        if not pre:
            sync.wait_ge(s_ev.h, e3 + P3N * EVC)
            s_sty.inc(sync.dma_start(out=y_bnc[(P3N - 1) * L:P3N * L, :],
                                     in_=ych[:]), 16)
    sy_all = s_sty.n

    def trwait(eng, c):
        eng.wait_ge(s_trv[c % 2].h, tr3[c % 2] + 32 * (c // 2 + 1))

    @blk.tensor
    def _(tn):
        tn.wait_ge(s_ev.h, e_carry0)
        tn.wait_ge(s_ev.h, e_trans)
        if not pre:
            tn.wait_ge(s_bcf.h, x_bcf)
        for c in range(P3N):
            trwait(tn, c)
            for o in range(NOCT):
                g = o // 2
                if not pre:
                    for qi in range(2):
                        q = 2 * o + qi
                        rr = q // 2
                        tn.wait_ge(s_csv[rr % 2].h,
                                   cs3[rr % 2] + 16 * (c * 8 + rr // 2 + 1))
                        pDp = (pD, pE)[q % 2]
                        gq = c * 32 + q
                        if gq >= 2:
                            pc, pq = divmod(gq - 2, 32)
                            tn.wait_ge(s_ac.h, ac_quad(pc, pq))
                        if q % 4 == 0:
                            gg = c * 8 + g
                            pCB = (pB, pC)[gg % 2]
                            if gg >= 2:
                                pc2, pq2 = divmod((gg - 2) * 4 + 3, 32)
                                tn.wait_ge(s_ev.h, ev_quad(pc2, pq2))
                            tn.matmul(
                                pCB[:, 0:128],
                                Bfeat[:, g * T + c * L:g * T + (c + 1) * L],
                                Cfeat[:, g * T + c * L:g * T + (c + 1) * L],
                                start=True, stop=True)
                        tn.matmul(pDp[:, 0:512], onesf_s[:],
                                  csrow[rr % 2][0:1,
                                                (q % 2) * 512:(q % 2) * 512 + 512],
                                  start=True, stop=False)
                        s_mm.inc(tn.matmul(
                            pDp[:, 0:512].rearrange(
                                "p (hh ll) -> p hh ll", hh=4),
                            maskT1_s[:],
                            maskT2_s[:].unsqueeze(1).broadcast_to(
                                [128, 4, 128]),
                            start=False, stop=True))
                    if c == 0:
                        tn.wait_ge(s_ev.h, e_carry0)
                    else:
                        tn.wait_ge(s_ev.h, ev_oct(c - 1, o, 5))
                    if (c, o) != (0, 0):
                        pc, po = ((c, o - 1) if o > 0 else (c - 1, NOCT - 1))
                        tn.wait_ge(s_ev.h, ev_oct(pc, po, 4))
                    for hh in range(8):
                        h = o * 8 + hh
                        mi = tn.matmul(
                            pG[:, hh * 64:(hh + 1) * 64],
                            Cfeat[:, g * T + c * L:g * T + (c + 1) * L],
                            carry[:, h * 64:(h + 1) * 64],
                            start=(hh == 0), stop=(hh == 7))
                    s_mm.inc(mi)
                    tn.wait_ge(s_ev.h, ev_oct(c, o, 3))
                    for hh in range(8):
                        h = o * 8 + hh
                        tn.matmul(pG[:, hh * 64:(hh + 1) * 64],
                                  scob[:, hh * 128:(hh + 1) * 128],
                                  xTk[c % 2][:, h * 64:(h + 1) * 64],
                                  start=(hh == 0), stop=False)
                    s_mm.inc(tn.matmul(pG[:, 0:512], identb_s[:], ytmp[:],
                                       start=False, stop=True))
                tn.wait_ge(s_ev.h, ev_xdd(c))
                if (c, o) != (0, 0):
                    pc, po = ((c, o - 1) if o > 0 else (c - 1, NOCT - 1))
                    tn.wait_ge(s_ev.h, ev_oct(pc, po, 5))
                for hh in range(8):
                    h = o * 8 + hh
                    mi = tn.matmul(pH[:, hh * 64:(hh + 1) * 64],
                                   BTk[c % 2][:, g * SS:(g + 1) * SS],
                                   xddT[:, h * 64:(h + 1) * 64],
                                   start=(hh == 0), stop=(hh == 7))
                s_mm.inc(mi)
                tn.wait_ge(s_cdr.h, cd3 + 16 * (c + 1))
                rhs = cdrow[0:1, o * 8:(o + 1) * 8].unsqueeze(2)
                rhs = rhs.broadcast_to([1, 8, 64])
                s_mm.inc(tn.matmul(pA[:, 0:512], onesf_s[:], rhs,
                                   start=True, stop=True))

    if not pre:
        @blk.scalar
        def _(sc):
            for c in range(P3N):
                for o in range(NOCT):
                    for qi in range(2):
                        q = 2 * o + qi
                        sc.wait_ge(s_mm.h, mm_quad(c, q))
                        go = c * 16 + o
                        if go >= 1:
                            pc, po = divmod(go - 1, 16)
                            sc.wait_ge(s_ev.h, ev_quad(pc, po * 2 + 1))
                        pDp = (pD, pE)[q % 2]
                        for h4 in range(4):
                            h = q * 4 + h4
                            col = qi * 512 + h4 * 128
                            ai = sc.activation(
                                expD[:, col:col + 128],
                                pDp[:, h4 * 128:(h4 + 1) * 128], AF.Exp,
                                bias=vT_a[:, c * L + h:c * L + h + 1])
                        s_ac.inc(ai)

    @blk.vector
    def _(v):
        for c in range(P3N):
            trwait(v, c)
            v.wait_ge(s_ev.h, e_trans)
            if c >= 1:
                v.wait_ge(s_mm.h, m3 + c * MMC)
            g2ap = g2T_a[:, c * L:(c + 1) * L].unsqueeze(2)
            g2ap = g2ap.broadcast_to([128, 128, 64])
            s_ev.inc(v.tensor_tensor(
                xddT[:].rearrange("p (hh pp) -> p hh pp", hh=128),
                xTk[c % 2][:].rearrange("p (hh pp) -> p hh pp", hh=128),
                g2ap, ALU.mult))
            for o in range(NOCT):
                if not pre:
                    for qi in range(2):
                        q = 2 * o + qi
                        v.wait_ge(s_ac.h, ac_quad(c, q))
                        if (c, o) != (0, 0):
                            pc0, po0 = ((c, o - 1) if o > 0
                                        else (c - 1, NOCT - 1))
                            v.wait_ge(s_mm.h, mm_oct(pc0, po0, 4))
                        gg = c * 8 + o // 2
                        pCB = (pB, pC)[gg % 2]
                        cbap = pCB[:, 0:128].unsqueeze(1).broadcast_to(
                            [128, 4, 128])
                        s_ev.inc(v.tensor_tensor(
                            scob[:, qi * 512:qi * 512 + 512].rearrange(
                                "p (hh ll) -> p hh ll", hh=4),
                            expD[:, qi * 512:qi * 512 + 512].rearrange(
                                "p (hh ll) -> p hh ll", hh=4),
                            cbap, ALU.mult))
                    v.wait_ge(s_mm.h, mm_oct(c, o, 3))
                    eap = ecsT_a[:, c * L + o * 8:c * L + (o + 1) * 8]
                    eap = eap.unsqueeze(2).broadcast_to([128, 8, 64])
                    s_ev.inc(v.tensor_tensor(
                        ytmp[:].rearrange("p (hh pp) -> p hh pp", hh=8),
                        pG[:, 0:512].rearrange("p (hh pp) -> p hh pp", hh=8),
                        eap, ALU.mult))
                    v.wait_ge(s_mm.h, mm_oct(c, o, 4))
                    if o == 0 and c >= 1:
                        v.wait_ge(s_sty.h, sy3 + 16 * c)
                    s_ev.inc(v.tensor_copy(ych[:, o * 512:(o + 1) * 512],
                                           pG[:, 0:512]))
                    v.wait_ge(s_mm.h, mm_oct(c, o, 6))
                    cols = slice(o * 512, (o + 1) * 512)
                    v.tensor_tensor(carry[:, cols], carry[:, cols],
                                    pA[:, 0:512], ALU.mult)
                    s_ev.inc(v.tensor_tensor(carry[:, cols], carry[:, cols],
                                             pH[:, 0:512], ALU.add))
                else:
                    v.wait_ge(s_mm.h, mm_oct(c, o, 2))
                    cols = slice(o * 512, (o + 1) * 512)
                    v.tensor_tensor(carry[:, cols], carry[:, cols],
                                    pA[:, 0:512], ALU.mult)
                    s_ev.inc(v.tensor_tensor(carry[:, cols], carry[:, cols],
                                             pH[:, 0:512], ALU.add))
    e_p3 = s_ev.n
    dump("carry_out", carry[:, 0:1024])
    dump("ych", ych[:] if not pre else carry[:, 0:1])

    if pre:
        @blk.sync
        def _(sync):
            sync.wait_ge(s_ev.h, e_p3)
            s_out.inc(sync.dma_start(out=contrib_o[:, :], in_=carry[:]), 16)
            s_out.inc(sync.dma_start(out=logG_o[:, :], in_=logG[:]), 16)
            s_out.inc(sync.dma_start(out=qpart_o[:, :], in_=qp_s[:]), 16)
        return finish()

    if STOP_AFTER == "p3" and not pre:
        return finish()

    al.free("ytmp", "scob", "expD", "csrow1", "csrow0", "ych",
            "BTk1", "BTk0", "xddT", "xTk1", "xTk0", "carry",
            "Cfeat", "Bfeat", "g2T_a", "vT_a", "ecsT_a", "h_lora")

    xres = al("xres", [128, T])
    oevac = [al(f"oevac{i}", [128, T]) for i in range(2)]
    wb4 = [al(f"wb4_{i}", [128, 2048], BF16) for i in range(2)]
    ygn = al("ygn", [128, OC_X * T], BF16)
    ytc = al("ytc", [128, T], BF16)
    gch = al("gch", [128, T], BF16)
    xfch = al("xfch", [128, T], BF16)
    sgb = al("sgb", [128, T], BF16)
    sqb = al("sqb", [128, T], BF16)
    msrow = al("msrow", [1, 1024])
    rsrow = al("rsrow", [1, 1024])

    m4, e4, a4 = s_mm.n, s_ev.n, s_ac.n
    t4, g4, f4 = s_try.n, s_gch.n, s_xfc.n

    def v_yg(r):
        return e4 + 18 * (r // 8) + 2 * (r % 8) + 1

    def v_stt(r):
        return v_yg(r) + 1

    def v_msev(g):
        return e4 + 18 * g + 17

    def v_passB(g):
        return e4 + 18 * (g + 1)

    def a_silu(r):
        return a4 + 17 * (r // 8) + 2 * (r % 8) + 1

    def a_sq(r):
        return a_silu(r) + 1

    def a_rs(g):
        return a4 + 17 * (g + 1)

    def m_ms(r):
        return m4 + 10 * (r // 8) + (r % 8) + 1

    def m_bc2(g):
        return m4 + 10 * (g + 1)

    @blk.sync
    def _(sync):
        sync.wait_ge(s_sty.h, sy_all)
        for r in range(OC_X):
            if r >= 1:
                sync.wait_ge(s_ev.h, v_stt(r - 1))
            s_try.inc(sync.dma_start_transpose(
                ytc[:], y_bnc[:, r * 128:(r + 1) * 128]), 16)
            s_gch.inc(sync.dma_start(out=gch[:], in_=gate_bnc[r, :, :]), 16)
            s_xfc.inc(sync.dma_start(out=xfch[:],
                                     in_=x_bnc[r * 128:(r + 1) * 128, :]), 16)

    @blk.scalar
    def _(sc):
        for r in range(OC_X):
            g, gi = divmod(r, 8)
            sc.wait_ge(s_gch.h, g4 + 16 * (r + 1))
            if r >= 1:
                sc.wait_ge(s_ev.h, v_stt(r - 1))
            s_ac.inc(sc.activation(sgb[:], gch[:], AF.Silu))
            sc.wait_ge(s_ev.h, v_stt(r))
            if r >= 1:
                sc.wait_ge(s_mm.h, m_ms(r - 1))
            s_ac.inc(sc.activation(sqb[:], ygn[:, r * T:(r + 1) * T],
                                   AF.Square))
            if gi == 7:
                sc.wait_ge(s_ev.h, v_msev(g))
                sc.activation(rsrow[0:1, :], msrow[0:1, :], AF.Ln,
                              bias=fc(C_EPS)[0:1, :],
                              scale=fc(C_INV1024)[0:1, :])
                s_ac.inc(sc.activation(rsrow[0:1, :], rsrow[0:1, :], AF.Exp,
                                       scale=fc(C_NHALF)[0:1, :]))

    onesb_col = maskT1_s[:, 127:128]

    @blk.tensor
    def _(tn):
        for r in range(OC_X):
            g, gi = divmod(r, 8)
            tn.wait_ge(s_ac.h, a_sq(r))
            if gi == 0 and g >= 1:
                tn.wait_ge(s_ev.h, v_msev(g - 1))
            for i in range(2):
                rr = P0ROW[i]
                mi = tn.matmul(pA[rr:rr + 1, 0:512], onesb_col,
                               sqb[:, i * 512:(i + 1) * 512],
                               start=(gi == 0), stop=(gi == 7))
            s_mm.inc(mi)
            if gi == 7:
                tn.wait_ge(s_ac.h, a_rs(g))
                if g >= 1:
                    tn.wait_ge(s_ev.h, v_passB(g - 1))
                for i in range(2):
                    s_mm.inc(tn.matmul(
                        (pB if i == 0 else pC)[:, 0:512], onesf_s[:],
                        rsrow[0:1, i * 512:(i + 1) * 512],
                        start=True, stop=True))

    @blk.vector
    def _(v):
        for r in range(OC_X):
            g, gi = divmod(r, 8)
            v.wait_ge(s_try.h, t4 + 16 * (r + 1))
            v.wait_ge(s_xfc.h, f4 + 16 * (r + 1))
            v.wait_ge(s_ac.h, a_silu(r))
            s_ev.inc(v.scalar_tensor_tensor(
                ygn[:, r * T:(r + 1) * T], xfch[:], Df_s[:, r:r + 1],
                ytc[:], ALU.mult, ALU.add))
            s_ev.inc(v.tensor_tensor(ygn[:, r * T:(r + 1) * T],
                                     ygn[:, r * T:(r + 1) * T], sgb[:],
                                     ALU.mult))
            if gi == 7:
                v.wait_ge(s_mm.h, m_ms(r))
                v.tensor_copy(msrow[0:1, 0:512], pA[0:1, 0:512])
                s_ev.inc(v.tensor_copy(msrow[0:1, 512:1024],
                                       pA[32:33, 0:512]))
                v.wait_ge(s_mm.h, m_bc2(g))
                for gj in range(8):
                    rq = g * 8 + gj
                    for i in range(2):
                        p = pB if i == 0 else pC
                        vi = v.scalar_tensor_tensor(
                            ygn[:, rq * T + i * 512:rq * T + (i + 1) * 512],
                            ygn[:, rq * T + i * 512:rq * T + (i + 1) * 512],
                            nw_s[:, rq:rq + 1], p[:, 0:512],
                            ALU.mult, ALU.mult)
                s_ev.inc(vi)
    e_ygn = s_ev.n
    dump("ygn", ygn[:, 0:T])
    dump("ygn7", ygn[:, (OC_X - 1) * T:OC_X * T])
    dump("ytc63", ytc[:])
    dump("gch63", gch[:])
    dump("xfch63", xfch[:])
    dump("sgb63", sgb[:])
    dump("msrow7", msrow[:])
    dump("rsrow7", rsrow[:])
    if STOP_AFTER == "p3b" and not pre:
        return finish()

    al.free("rsrow", "msrow", "sqb", "sgb", "xfch", "gch", "ytc")

    m5, e5 = s_mm.n, s_ev.n
    w5 = [s_wv[0].n, s_wv[1].n]
    x5 = s_xre.n
    so5 = [s_stv[0].n, s_stv[1].n]

    def m_sub(oc, qq):
        return m5 + 4 * oc + qq + 1

    @blk.gpsimd
    def _(gp):
        gp.wait_ge(s_ev.h, e_ygn)
        for oc in range(NFC):
            for qq in range(4):
                s4 = 4 * oc + qq
                if s4 >= 2:
                    gp.wait_ge(s_mm.h, m5 + s4 - 1)
                s_wv[s4 % 2].inc(gp.dma_start(
                    out=wb4[s4 % 2][:],
                    in_=w_out[oc, :, qq * 16:(qq + 1) * 16, :].rearrange(
                        "p a b -> p (a b)")), 16)

    @blk.sync
    def _(sync):
        for oc in range(NFC):
            if oc >= 1:
                sync.wait_ge(s_ev.h, e5 + 2 * oc)
                s_stv[(oc - 1) % 2].inc(sync.dma_start(
                    out=outpart[oc - 1, :, :], in_=oevac[(oc - 1) % 2][:]), 16)
            s_xre.inc(sync.dma_start(
                out=xres[:], in_=xT[oc * 128:(oc + 1) * 128, HALO:NT]), 16)
        sync.wait_ge(s_ev.h, e5 + 2 * NFC)
        s_stv[(NFC - 1) % 2].inc(sync.dma_start(
            out=outpart[NFC - 1, :, :], in_=oevac[(NFC - 1) % 2][:]), 16)

    @blk.tensor
    def _(tn):
        tn.wait_ge(s_ev.h, e_ygn)
        for oc in range(NFC):
            if oc >= 1:
                tn.wait_ge(s_ev.h, e5 + 2 * oc)
            for qq in range(4):
                s4 = 4 * oc + qq
                tn.wait_ge(s_wv[s4 % 2].h, w5[s4 % 2] + 16 * (s4 // 2 + 1))
                for i, (aa, bb) in enumerate([(0, 512), (512, 1024)]):
                    p = (pA, pB)[i]
                    for rc in range(16):
                        rg = qq * 16 + rc
                        mi = tn.matmul(
                            p[:, 0:512],
                            wb4[s4 % 2][:, rc * 128:(rc + 1) * 128],
                            ygn[:, rg * T + aa:rg * T + bb],
                            start=(qq == 0 and rc == 0),
                            stop=(qq == 3 and rc == 15))
                s_mm.inc(mi)

    @blk.vector
    def _(v):
        for oc in range(NFC):
            v.wait_ge(s_xre.h, x5 + 16 * (oc + 1))
            v.wait_ge(s_mm.h, m_sub(oc, 3))
            if oc >= 2:
                v.wait_ge(s_stv[oc % 2].h,
                          so5[oc % 2] + 16 * ((oc - 2) // 2 + 1))
            for i in range(2):
                p = (pA, pB)[i]
                s_ev.inc(v.scalar_tensor_tensor(
                    oevac[oc % 2][:, i * 512:(i + 1) * 512], p[:, 0:512],
                    1.0, xres[:, i * 512:(i + 1) * 512], ALU.mult, ALU.add))
    dump("outpart0", oevac[0][:, 0:T])
    if STOP_AFTER == "p4" and not pre:
        return finish()

    al.free("ygn", "wb4_1", "wb4_0")

    xnT = al("xnT2", [128, NFC * NT], BF16)
    xTc = [al(f"xTc2_{i}", [128, NT]) for i in range(2)]
    xsq = al("xsq2", [128, NT])
    rb_bc = al("rb_bc2", [128, NT])
    mb_bc = al("mb_bc2", [128, NT])
    xnT_ev2 = emit_ln_stats_and_xnT(False)

    wb5 = [al(f"wb5_{i}", [128, 4608], BF16) for i in range(2)]
    hcs = al("hcs", [128, 4 * T], BF16)
    opart = [al(f"opart{i}", [128, T]) for i in range(2)]

    m6, e6 = s_mm.n, s_ev.n
    w6 = [s_wv[0].n, s_wv[1].n]
    op6 = [s_opv[0].n, s_opv[1].n]
    ou6 = [s_ouv[0].n, s_ouv[1].n]

    @blk.sync
    def _(sync):
        sync.wait_ge(s_stv[0].h, s_stv[0].n)
        sync.wait_ge(s_stv[1].h, s_stv[1].n)
        sync.wait_ge(s_stg.h, s_stg.n)
        s_hcs.inc(sync.dma_start(out=hcs[:], in_=hc_bnc[:, :]), 16)
        for oc in range(NFC):
            if oc >= 2:
                sync.wait_ge(s_ev.h, e6 + 2 * (oc - 1))
                s_ouv[oc % 2].inc(sync.dma_start(
                    out=outT[(oc - 2) * 128:(oc - 1) * 128, :],
                    in_=oevac[oc % 2][:]), 16)
            s_opv[oc % 2].inc(sync.dma_start(out=opart[oc % 2][:],
                                             in_=outpart[oc, :, :]), 16)
        for oc in (NFC - 2, NFC - 1):
            sync.wait_ge(s_ev.h, e6 + 2 * (oc + 1))
            s_ouv[oc % 2].inc(sync.dma_start(
                out=outT[oc * 128:(oc + 1) * 128, :],
                in_=oevac[oc % 2][:]), 16)
    hc6 = s_hcs.n

    @blk.gpsimd
    def _(gp):
        gp.wait_ge(s_ev.h, xnT_ev2)
        for oc in range(NFC):
            if oc >= 2:
                gp.wait_ge(s_mm.h, m6 + 2 * (oc - 1))
            s_wv[oc % 2].inc(gp.dma_start(
                out=wb5[oc % 2][:, 0:4096],
                in_=w_org[oc, :, :, :].rearrange("p a b -> p (a b)")), 16)
            s_wv[oc % 2].inc(gp.dma_start(
                out=wb5[oc % 2][:, 4096:4608],
                in_=w_lup[oc, :, :, :].rearrange("p a b -> p (a b)")), 16)

    @blk.tensor
    def _(tn):
        tn.wait_ge(s_ev.h, xnT_ev2)
        tn.wait_ge(s_hcs.h, hc6)
        for oc in range(NFC):
            tn.wait_ge(s_wv[oc % 2].h, w6[oc % 2] + 32 * (oc // 2 + 1))
            for i, (aa, bb) in enumerate([(HALO, HALO + 512),
                                          (HALO + 512, NT)]):
                p = (pA, pB)[i]
                idx = 2 * oc + i
                if idx >= 2:
                    tn.wait_ge(s_ev.h, e6 + idx - 1)
                for fc2 in range(NFC):
                    tn.matmul(p[:, 0:512],
                              wb5[oc % 2][:, fc2 * 128:(fc2 + 1) * 128],
                              xnT[:, fc2 * NT + aa:fc2 * NT + bb],
                              start=(fc2 == 0), stop=False)
                for j in range(4):
                    mi = tn.matmul(
                        p[:, 0:512],
                        wb5[oc % 2][:, 4096 + j * 128:4096 + (j + 1) * 128],
                        hcs[:, j * T + aa - HALO:j * T + bb - HALO],
                        start=False, stop=(j == 3))
                s_mm.inc(mi)

    @blk.vector
    def _(v):
        for oc in range(NFC):
            v.wait_ge(s_opv[oc % 2].h, op6[oc % 2] + 16 * (oc // 2 + 1))
            if oc >= 2:
                v.wait_ge(s_ouv[oc % 2].h,
                          ou6[oc % 2] + 16 * ((oc - 2) // 2 + 1))
            for i in range(2):
                p = (pA, pB)[i]
                v.wait_ge(s_mm.h, m6 + 2 * oc + i + 1)
                s_ev.inc(v.scalar_tensor_tensor(
                    oevac[oc % 2][:, i * 512:(i + 1) * 512], p[:, 0:512],
                    1.0, opart[oc % 2][:, i * 512:(i + 1) * 512],
                    ALU.mult, ALU.add))

    return finish()


def _blocked(wT):
    K, O = wT.shape
    return np.ascontiguousarray(
        wT.reshape(K // 128, 128, O // 128, 128).transpose(2, 1, 0, 3))


def kernel(L0_attn_hidden_states, L0_attn_in_proj_w, L0_attn_conv_w,
           L0_attn_conv_b, L0_attn_dt_bias, L0_attn_D, L0_attn_norm_w,
           L0_attn_out_proj_w, L0_ffn_W_org, L0_ffn_lora_down, L0_ffn_lora_up,
           L0_ffn_lora_keys, L0_ffn_Wq):
    x = np.asarray(L0_attn_hidden_states)

    w_in_b = _blocked(np.ascontiguousarray(np.asarray(L0_attn_in_proj_w).T))
    w_out_b = _blocked(np.ascontiguousarray(np.asarray(L0_attn_out_proj_w).T))
    w_org_b = _blocked(np.ascontiguousarray(np.asarray(L0_ffn_W_org).T))
    w_ldn_b = _blocked(np.ascontiguousarray(
        np.asarray(L0_ffn_lora_down).transpose(1, 0, 2).reshape(DM, 512)))
    w_lup_b = _blocked(np.ascontiguousarray(
        np.asarray(L0_ffn_lora_up).reshape(512, DM)))
    w_wq_b = _blocked(np.ascontiguousarray(np.asarray(L0_ffn_Wq).T))
    keysT_h = np.ascontiguousarray(np.asarray(L0_ffn_lora_keys).T)
    convw_h = np.ascontiguousarray(
        np.asarray(L0_attn_conv_w)[:, 0, :].reshape(NCV, 128, 4)
        .transpose(1, 0, 2).reshape(128, NCV * 4))
    convb_h = np.ascontiguousarray(
        np.asarray(L0_attn_conv_b).reshape(NCV, 128).T)
    dtb_h = np.ascontiguousarray(np.asarray(L0_attn_dt_bias).reshape(128, 1))
    A_h = -np.arange(1, NH + 1, dtype=np.float32).reshape(128, 1)
    D_feat_h = np.ascontiguousarray(
        np.repeat(np.asarray(L0_attn_D), HD).reshape(OC_X, 128).T)
    normw_h = np.ascontiguousarray(
        np.asarray(L0_attn_norm_w).reshape(OC_X, 128).T)
    identf_h = np.eye(128, dtype=np.float32)
    identb_h = np.eye(128).astype(BF)
    k_idx = np.arange(128)
    maskT1_h = (k_idx[:, None] <= k_idx[None, :]).astype(BF)
    maskT2_h = ((k_idx[:, None] > k_idx[None, :]).astype(np.float32)
                * -1e30).astype(BF)
    onesf_h = np.ones((1, 128), np.float32)
    onescol_h = np.ones((128, 1), np.float32)
    fcs_h = np.tile(np.array([EPS, -0.5, -1.0, 1.0 / DM, DT_MAX, DT_MIN,
                              1.0 / (DM * np.sqrt(128.0)), 1.0 / 1024.0],
                             np.float32).reshape(1, 8), (128, 1))

    common = dict(w_in=w_in_b, w_ldn=w_ldn_b, w_wq=w_wq_b, convw=convw_h,
                  convb=convb_h, dt_bias=dtb_h, A_col=A_h, identb=identb_h,
                  identf=identf_h, maskT1=maskT1_h, maskT2=maskT2_h,
                  onesf=onesf_h, onescol=onescol_h, fcs=fcs_h,
                  onesm=np.ones((128, 8), np.float32))

    def xT_of(core):
        b, qd = divmod(core, 4)
        start = qd * T
        buf = np.zeros((NT, DM), np.float32)
        lo = max(0, start - HALO)
        buf[HALO - (start - lo):] = x[b, lo:start + T]
        return np.ascontiguousarray(buf.T)

    xTs = [xT_of(c) for c in range(8)]

    nc1 = build("pre")
    in1 = [dict(common, xT=xTs[c]) for c in range(8)]
    r1 = run_bass_kernel_spmd(nc1, in1, core_ids=list(range(8))).results
    globals()["LAST_R1"] = r1
    contribs_all = [np.asarray(r["contrib_o"]) for r in r1]
    glogs_all = [np.asarray(r["logG_o"])[:, 0] for r in r1]
    qparts_all = [np.asarray(r["qpart_o"])[:, 0] for r in r1]

    contribs_in, glogs_in, qparts_in, ksels, msels = [], [], [], [], []
    for c in range(8):
        b, m = divmod(c, 4)
        grp = range(4 * b, 4 * b + 4)
        contribs_in.append(np.ascontiguousarray(
            np.stack([contribs_all[j] for j in grp])))
        glogs_in.append(np.ascontiguousarray(
            np.stack([glogs_all[j] for j in grp])))
        qparts_in.append(np.ascontiguousarray(
            np.stack([qparts_all[j] for j in grp])))
        ks = np.zeros((3, 4), np.float32)
        ms = np.zeros(3, np.float32)
        for j in range(m):
            ms[j] = 1.0
            for k in range(j + 1, m):
                ks[j, k] = 1.0
        ksels.append(np.ascontiguousarray(np.tile(ks.reshape(1, 12),
                                                  (128, 1))))
        msels.append(np.ascontiguousarray(np.tile(ms.reshape(1, 3),
                                                  (128, 1))))

    nc2 = build("full")
    common2 = dict(common, w_org=w_org_b, w_out=w_out_b, w_lup=w_lup_b,
                   keysT=keysT_h, D_feat=D_feat_h, normw=normw_h)
    in2 = [dict(common2, xT=xTs[c], contribs=contribs_in[c],
                glogs=glogs_in[c], qparts=qparts_in[c], kselb=ksels[c],
                mselb=msels[c])
           for c in range(8)]
    r2 = run_bass_kernel_spmd(nc2, in2, core_ids=list(range(8))).results
    globals()["LAST_R2"] = r2

    out = np.zeros((2, 4096, DM), np.float32)
    for c in range(8):
        b, qd = divmod(c, 4)
        out[b, qd * T:(qd + 1) * T, :] = np.asarray(r2[c]["outT"]).T
    return out


if __name__ == "__main__":
    build("pre")
    print("pre build ok")
    build("full")
    print("full build ok")
